# version 02
# baseline (speedup 1.0000x reference)
import numpy as np
import ml_dtypes
import concourse.bass as bass
import concourse.mybir as mybir
from concourse.alu_op_type import AluOpType as OP
from concourse.bass_utils import run_bass_kernel_spmd
from concourse.bass import IndirectOffsetOnAxis

F32 = mybir.dt.float32
BF16 = mybir.dt.bfloat16
U32 = mybir.dt.uint32
AF = mybir.ActivationFunctionType

NC = 8
T = 16384
D = 2048
TSH = T // NC
DSH = D // NC
RNK = 64
NST = 16
NEXP = 32
IDIM = 1024
EPS = 1e-5

ENG = ("pe", "dve", "act", "pool", "sync")
QS = ("qsp", "qact", "qdve", "qpool")


class Buf:
    def __init__(self):
        self.w = None
        self.readers = []

    def wdeps(self):
        return ([self.w] if self.w else []) + self.readers

    def rdeps(self):
        return [self.w] if self.w else []


class Ctx:

    def __init__(self, nc):
        self.nc = nc
        self.ops = {e: [] for e in ENG}
        self.cnt = {s: 0 for s in list(ENG) + list(QS)}
        self.extra_sems = []
        self.waited = {}
        self.sems = {}

    def add_qsem(self, name):
        self.extra_sems.append(name)
        self.cnt[name] = 0

    def _q_for(self, eng):
        return {"sync": "qsp", "act": "qact", "dve": "qdve", "pool": "qpool"}[eng]

    def deps(self, reads=(), writes=()):
        w = []
        for b in reads:
            w += b.rdeps()
        for b in writes:
            w += b.wdeps()
        return w

    def commit(self, tok, reads=(), writes=()):
        for b in writes:
            b.w = tok
            b.readers = []
        for b in reads:
            b.readers.append(tok)

    def op(self, eng, fn, reads=(), writes=(), dma=False, extra=(), drain=False,
           q=None):
        sig = (q if q else self._q_for(eng)) if dma else eng
        amt = 16 if dma else 1
        waits = self.deps(reads, writes) + list(extra)
        need = []
        for (pname, pcount) in waits:
            key = (eng, pname)
            if self.waited.get(key, 0) < pcount:
                self.waited[key] = pcount
                need.append((pname, pcount))
        self.cnt[sig] += amt
        target = self.cnt[sig]

        def closure(h, need=tuple(need), sig=sig, amt=amt, fn=fn, drain=drain):
            for (pname, pcount) in need:
                h.wait_ge(self.sems[pname], pcount)
            bi = fn(h)
            if drain:
                h.drain().then_inc(self.sems[sig], amt)
            else:
                bi.then_inc(self.sems[sig], amt)

        self.ops[eng].append(closure)
        tok = (sig, target)
        self.commit(tok, reads, writes)
        return tok

    def fence(self):
        return [(k, v) for k, v in self.cnt.items() if v > 0]

    def finalize(self, final_tokens):
        nc = self.nc
        import contextlib
        with contextlib.ExitStack() as stk:
            for s in list(ENG) + list(QS) + self.extra_sems:
                self.sems[s] = stk.enter_context(nc.semaphore("sem_" + s))
            blk = stk.enter_context(nc.Block())
            for e, hname in (("pe", "tensor"), ("dve", "vector"), ("act", "scalar"),
                             ("pool", "gpsimd"), ("sync", "sync")):
                ops = self.ops[e]
                if not ops and e != "sync":
                    continue

                def body(h, ops=ops, e=e):
                    for f in ops:
                        f(h)
                    if e == "sync":
                        for (pname, pcount) in final_tokens:
                            h.wait_ge(self.sems[pname], pcount)

                getattr(blk, hname)(body)



def build_L1():
    nc = bass.Bass("TRN2", target_bir_lowering=False)
    x = nc.dram_tensor("x", [TSH, D], F32, kind="ExternalInput").ap()
    xT = nc.dram_tensor("xT", [D, TSH], F32, kind="ExternalInput").ap()
    wxT = nc.dram_tensor("wxT", [D, 96], F32, kind="ExternalInput").ap()
    o_xn = nc.dram_tensor("o_xn", [TSH, D], F32, kind="ExternalOutput").ap()
    o_xnb = nc.dram_tensor("o_xnb", [TSH, D], BF16, kind="ExternalOutput").ap()
    o_xp = nc.dram_tensor("o_xp", [96, TSH], F32, kind="ExternalOutput").ap()
    statsT = nc.dram_tensor("statsT", [2, TSH], F32).ap()
    w1d = nc.dram_tensor("w1d", [96], F32).ap()

    c = Ctx(nc)
    NT = TSH // 128
    import contextlib
    with contextlib.ExitStack() as stk:
        sbuf = lambda n, s, d: stk.enter_context(nc.sbuf_tensor(n, s, d))
        psum = lambda n, s, d: stk.enter_context(nc.psum_tensor(n, s, d))
        xt = [sbuf(f"xt{i}", [128, D], F32) for i in range(2)]
        xnt = [sbuf(f"xnt{i}", [128, D], F32) for i in range(2)]
        xnb = sbuf("xnb", [128, D], BF16)
        bn6 = sbuf("bn6", [128, 24], F32)
        mv = sbuf("mv", [128, 2], F32)
        st2 = sbuf("st2", [128, 2], F32)
        lnv = sbuf("lnv", [128, 1], F32)
        wxs = sbuf("wxs", [128, 16, 96], F32)
        onesv = sbuf("onesv", [128, 1], F32)
        w1col = sbuf("w1col", [128, 1], F32)
        w1row = sbuf("w1row", [1, 96], F32)
        xTs = [sbuf(f"xTs{i}", [128, 16, 512], F32) for i in range(2)]
        rrep = sbuf("rrep", [128, 512], F32)
        nrep = sbuf("nrep", [128, 512], F32)
        xpr = sbuf("xpr", [96, 512], F32)
        xps = sbuf("xps", [96, TSH], F32)
        psxp = psum("psxp", [128, 512], F32)
        psw1 = psum("psw1", [128, 96], F32)

        bXT = [Buf(), Buf()]
        bXN = [Buf(), Buf()]
        bXNB, bBN, bMV, bST2, bLNV = Buf(), Buf(), Buf(), Buf(), Buf()
        bWX, bXPS, bXPR = Buf(), Buf(), Buf()
        bXTS = [Buf(), Buf()]
        bPXP, bPW1 = Buf(), Buf()
        bONE, bW1C, bW1R, bW1D = Buf(), Buf(), Buf(), Buf()
        bRR, bNR, bSTT = Buf(), Buf(), Buf()

        ceps = sbuf("ceps", [128, 1], F32)
        bEPS = Buf()
        nc.const_aps.register(F32, EPS, ceps[:, 0:1])
        for qn in ("qrr", "qnr", "qw1"):
            c.add_qsem(qn)
        c.op("dve", lambda h: h.memset(ceps[:], EPS), writes=[bEPS], drain=True)

        c.op("sync", lambda h: h.dma_start(
            out=wxs[:], in_=wxT.rearrange("(dt p) c -> p dt c", p=128)),
            writes=[bWX], dma=True)
        import os as _os
        _part = int(_os.environ.get("K_L1_PART", "3"))
        c.op("dve", lambda h: h.memset(onesv[:], 1.0), writes=[bONE], drain=True)
        wts = c.deps(reads=[bWX, bONE], writes=[bPW1])
        if _part >= 2:
            tok = None
            for dt in range(16):
                tok = c.op("pe", lambda h, dt=dt: h.matmul(
                    psw1[0:1, :], onesv[:], wxs[:, dt, :],
                    start=(dt == 0), stop=(dt == 15)), extra=wts if dt == 0 else ())
            c.commit(tok, reads=[bWX, bONE], writes=[bPW1])
            c.op("act", lambda h: h.copy(w1row[:], psw1[0:1, :]),
                 reads=[bPW1], writes=[bW1R], drain=True)
            c.op("sync", lambda h: h.dma_start(
                out=w1d.rearrange("(o f) -> o f", o=1), in_=w1row[:]),
                reads=[bW1R], writes=[bW1D], dma=True, q="qw1")
            c.op("sync", lambda h: h.dma_start(
                out=w1col[0:96, :], in_=w1d.rearrange("(f o) -> f o", o=1)),
                reads=[bW1D], writes=[bW1C], dma=True, q="qw1")

        for i in range(NT):
            ib = i % 2
            c.op("sync", lambda h, i=i, ib=ib: h.dma_start(
                out=xt[ib][:], in_=x[i * 128:(i + 1) * 128, :]),
                writes=[bXT[ib]], dma=True)
            for j in range(4):
                c.op("dve", lambda h, j=j, ib=ib: h.bn_stats(
                    bn6[:, j * 6:(j + 1) * 6], xt[ib][:, j * 512:(j + 1) * 512]),
                    reads=[bXT[ib]], writes=[bBN])
            c.op("dve", lambda h: h.bn_aggr(mv[:], bn6[:]),
                 reads=[bBN], writes=[bMV], drain=True)
            c.op("act", lambda h: h.activation(lnv[:], mv[:, 1:2], AF.Ln, bias=EPS),
                 reads=[bMV, bEPS], writes=[bLNV])
            c.op("act", lambda h: h.activation(st2[:, 0:1], lnv[:], AF.Exp,
                                               scale=-0.5),
                 reads=[bLNV], writes=[bST2], drain=True)
            c.op("dve", lambda h: h.scalar_tensor_tensor(
                st2[:, 1:2], mv[:, 0:1], -1.0, st2[:, 0:1],
                op0=OP.mult, op1=OP.mult), reads=[bMV, bST2], writes=[bST2],
                drain=True)
            c.op("sync", lambda h, i=i: h.dma_start(
                out=statsT[0, i * 128:(i + 1) * 128].rearrange("(p o) -> p o", o=1),
                in_=st2[:, 0:1]), reads=[bST2], writes=[bSTT], dma=True)
            c.op("sync", lambda h, i=i: h.dma_start(
                out=statsT[1, i * 128:(i + 1) * 128].rearrange("(p o) -> p o", o=1),
                in_=st2[:, 1:2]), reads=[bST2], writes=[bSTT], dma=True)
            c.op("dve", lambda h, ib=ib: h.tensor_scalar(
                xnt[ib][:], xt[ib][:], st2[:, 0:1], st2[:, 1:2],
                op0=OP.mult, op1=OP.add),
                reads=[bXT[ib], bST2], writes=[bXN[ib]], drain=True)
            c.op("sync", lambda h, i=i, ib=ib: h.dma_start(
                out=o_xn[i * 128:(i + 1) * 128, :], in_=xnt[ib][:]),
                reads=[bXN[ib]], dma=True)
            c.op("dve", lambda h, ib=ib: h.tensor_copy(xnb[:], xnt[ib][:]),
                 reads=[bXN[ib]], writes=[bXNB], drain=True)
            c.op("sync", lambda h, i=i: h.dma_start(
                out=o_xnb[i * 128:(i + 1) * 128, :], in_=xnb[:]),
                reads=[bXNB], dma=True)

        for g in range(TSH // 512 if _part >= 3 else 0):
            gb = g % 2
            t0 = g * 512
            c.op("sync", lambda h, t0=t0, gb=gb: h.dma_start(
                out=xTs[gb][:], in_=xT[:, t0:t0 + 512].rearrange(
                    "(dt p) t -> p dt t", p=128)), writes=[bXTS[gb]], dma=True)
            c.op("act", lambda h, t0=t0: h.dma_start(
                out=rrep[:], in_=statsT.partition_broadcast(128)[:, 0, t0:t0 + 512]),
                reads=[bSTT], writes=[bRR], dma=True, q="qrr")
            c.op("act", lambda h, t0=t0: h.dma_start(
                out=nrep[:], in_=statsT.partition_broadcast(128)[:, 1, t0:t0 + 512]),
                reads=[bSTT], writes=[bNR], dma=True, q="qnr")
            wts = c.deps(reads=[bXTS[gb], bWX], writes=[bPXP])
            tok = None
            for dt in range(16):
                tok = c.op("pe", lambda h, dt=dt, gb=gb: h.matmul(
                    psxp[0:96, :], wxs[:, dt, :], xTs[gb][:, dt, :],
                    start=(dt == 0), stop=(dt == 15)),
                    extra=wts if dt == 0 else ())
            c.commit(tok, reads=[bXTS[gb], bWX], writes=[bPXP])
            c.op("act", lambda h: h.copy(xpr[:], psxp[0:96, :]),
                 reads=[bPXP], writes=[bXPR], drain=True)
            c.op("dve", lambda h: h.tensor_tensor(
                xpr[:], xpr[:], rrep[0:96, :], op=OP.mult),
                reads=[bXPR, bRR], writes=[bXPR])
            c.op("dve", lambda h, g=g: h.scalar_tensor_tensor(
                xps[0:96, g * 512:(g + 1) * 512], nrep[0:96, :], w1col[0:96, 0:1],
                xpr[:], op0=OP.mult, op1=OP.add),
                reads=[bXPR, bNR, bW1C], writes=[bXPS], drain=True)

        c.op("sync", lambda h: h.dma_start(out=o_xp, in_=xps[0:96, :]),
             reads=[bXPS], dma=True)
        c.finalize([(qn, c.cnt[qn]) for qn in
                    ("qsp", "qact", "qdve", "qpool", "qrr", "qnr", "qw1")])
    return nc



def build_L2():
    nc = bass.Bass("TRN2", target_bir_lowering=False)
    xnd = nc.dram_tensor("xnd", [DSH, T], F32, kind="ExternalInput").ap()
    xp = nc.dram_tensor("xp", [96, T], F32, kind="ExternalInput").ap()
    wdtT = nc.dram_tensor("wdtT", [RNK, DSH], F32, kind="ExternalInput").ap()
    bdt = nc.dram_tensor("bdt", [128, DSH // 128, 1], F32, kind="ExternalInput").ap()
    alog = nc.dram_tensor("alog", [128, DSH // 128, NST], F32,
                          kind="ExternalInput").ap()
    dvec = nc.dram_tensor("dvec", [128, DSH // 128, 1], F32,
                          kind="ExternalInput").ap()
    o_s = nc.dram_tensor("o_s", [DSH, T], F32, kind="ExternalOutput").ap()

    SEG = 2048
    NSEG = T // SEG
    NDT = DSH // 128

    c = Ctx(nc)
    import contextlib
    with contextlib.ExitStack() as stk:
        sbuf = lambda n, s, d: stk.enter_context(nc.sbuf_tensor(n, s, d))
        psum = lambda n, s, d: stk.enter_context(nc.psum_tensor(n, s, d))
        xn_d = [sbuf(f"xn_d{i}", [128, NDT, SEG], F32) for i in range(2)]
        xps = [sbuf(f"xps{i}", [96, SEG], F32) for i in range(2)]
        wdts = sbuf("wdts", [RNK, DSH], F32)
        bdts = sbuf("bdts", [128, NDT, 1], F32)
        alogs = sbuf("alogs", [128, NDT, NST], F32)
        aneg = sbuf("aneg", [128, NDT, NST], F32)
        dvs = sbuf("dvs", [128, NDT, 1], F32)
        dtv = sbuf("dtv", [128, NDT, SEG], F32)
        uv = sbuf("uv", [128, NDT, SEG], F32)
        brep = [sbuf(f"brep{i}", [128, SEG], F32) for i in range(2)]
        crep = [sbuf(f"crep{i}", [128, SEG], F32) for i in range(2)]
        dAv = [sbuf(f"dAv{i}", [128, SEG], F32) for i in range(2)]
        dBx = sbuf("dBx", [128, SEG], F32)
        hbuf = sbuf("hbuf", [128, SEG], F32)
        htil = sbuf("htil", [128, SEG], F32)
        yacc = sbuf("yacc", [128, NDT, SEG], F32)
        sv = sbuf("sv", [128, NDT, SEG], F32)
        statev = sbuf("statev", [128, NDT, NST], F32)
        rl = sbuf("rl", [128, 512], F32)
        ab = sbuf("ab", [128, 512], F32)
        psdt = psum("psdt", [128, 512], F32)

        bXND = [Buf(), Buf()]
        bXPS = [Buf(), Buf()]
        bW, bB, bAL, bAN, bDV = Buf(), Buf(), Buf(), Buf(), Buf()
        bDT, bU, bY, bSV, bSTATE = Buf(), Buf(), Buf(), Buf(), Buf()
        bBR = [Buf(), Buf()]
        bCR = [Buf(), Buf()]
        bDA = [Buf(), Buf()]
        bDBX, bH, bHT = Buf(), Buf(), Buf()
        bRL, bAB = Buf(), Buf()
        bPDT = Buf()
        for qn in ("qb0", "qb1", "qc0", "qc1"):
            c.add_qsem(qn)

        c.op("sync", lambda h: h.dma_start(out=wdts[0:RNK, :], in_=wdtT),
             writes=[bW], dma=True)
        c.op("sync", lambda h: h.dma_start(out=bdts[:], in_=bdt),
             writes=[bB], dma=True)
        c.op("sync", lambda h: h.dma_start(out=alogs[:], in_=alog),
             writes=[bAL], dma=True)
        c.op("sync", lambda h: h.dma_start(out=dvs[:], in_=dvec),
             writes=[bDV], dma=True)
        c.op("act", lambda h: h.activation(aneg[:], alogs[:], AF.Exp),
             reads=[bAL], writes=[bAN], drain=True)
        c.op("dve", lambda h: h.tensor_scalar(
            aneg[:], aneg[:], -1.0, None, op0=OP.mult),
            reads=[bAN], writes=[bAN], drain=True)
        c.op("dve", lambda h: h.memset(statev[:], 0.0), writes=[bSTATE],
             drain=True)

        for s in range(NSEG):
            t0 = s * SEG
            sb = s % 2
            c.op("sync", lambda h, t0=t0, sb=sb: h.dma_start(
                out=xn_d[sb][:], in_=xnd[:, t0:t0 + SEG].rearrange(
                    "(dt p) t -> p dt t", p=128)), writes=[bXND[sb]], dma=True)
            c.op("sync", lambda h, t0=t0, sb=sb: h.dma_start(
                out=xps[sb][0:96, :], in_=xp[0:96, t0:t0 + SEG]),
                writes=[bXPS[sb]], dma=True)

            for dt in range(NDT):
                for fc in range(SEG // 512):
                    c.op("pe", lambda h, dt=dt, fc=fc, sb=sb: h.matmul(
                        psdt[:], wdts[0:RNK, dt * 128:(dt + 1) * 128],
                        xps[sb][0:RNK, fc * 512:(fc + 1) * 512],
                        start=True, stop=True),
                        reads=[bXPS[sb], bW], writes=[bPDT])
                    c.op("act", lambda h, dt=dt: h.activation(
                        rl[:], psdt[:], AF.Relu, bias=bdts[:, dt, 0:1]),
                        reads=[bPDT, bB], writes=[bRL])
                    c.op("act", lambda h, dt=dt: h.activation(
                        ab[:], psdt[:], AF.Abs, bias=bdts[:, dt, 0:1]),
                        reads=[bPDT, bB], writes=[bAB])
                    c.op("act", lambda h: h.activation(
                        ab[:], ab[:], AF.Exp, scale=-1.0),
                        reads=[bAB], writes=[bAB])
                    c.op("act", lambda h: h.activation(
                        ab[:], ab[:], AF.Ln, bias=1.0),
                        reads=[bAB], writes=[bAB], drain=True)
                    c.op("dve", lambda h, dt=dt, fc=fc: h.tensor_tensor(
                        dtv[:, dt, fc * 512:(fc + 1) * 512], rl[:], ab[:],
                        op=OP.add), reads=[bRL, bAB], writes=[bDT],
                        drain=(fc == 1))

            for dt in range(NDT):
                c.op("dve", lambda h, dt=dt, sb=sb: h.tensor_tensor(
                    uv[:, dt, :], dtv[:, dt, :], xn_d[sb][:, dt, :], op=OP.mult),
                    reads=[bDT, bXND[sb]], writes=[bU])

            c.op("dve", lambda h: h.memset(yacc[:], 0.0), writes=[bY])

            for n in range(NST):
                nb = n % 2
                c.op("sync", lambda h, n=n, nb=nb, t0=t0: h.dma_start(
                    out=brep[nb][:],
                    in_=xp.partition_broadcast(128)[:, 64 + n, t0:t0 + SEG]),
                    writes=[bBR[nb]], dma=True, q="qb%d" % nb)
                c.op("act", lambda h, n=n, nb=nb, t0=t0: h.dma_start(
                    out=crep[nb][:],
                    in_=xp.partition_broadcast(128)[:, 80 + n, t0:t0 + SEG]),
                    writes=[bCR[nb]], dma=True, q="qc%d" % nb)
                for dt in range(NDT):
                    c.op("act", lambda h, n=n, nb=nb, dt=dt: h.activation(
                        dAv[nb][:], dtv[:, dt, :], AF.Exp,
                        scale=aneg[:, dt, n:n + 1]),
                        reads=[bDT, bAN], writes=[bDA[nb]], drain=True)
                    c.op("dve", lambda h, nb=nb, dt=dt: h.tensor_tensor(
                        dBx[:], uv[:, dt, :], brep[nb][:], op=OP.mult),
                        reads=[bU, bBR[nb]], writes=[bDBX])
                    c.op("dve", lambda h, nb=nb, dt=dt, n=n:
                         h.tensor_tensor_scan(
                        hbuf[:], dAv[nb][:], dBx[:],
                        statev[:, dt, n:n + 1], op0=OP.mult, op1=OP.add),
                        reads=[bDA[nb], bDBX, bSTATE], writes=[bH])
                    c.op("dve", lambda h, dt=dt, n=n: h.tensor_copy(
                        statev[:, dt, n:n + 1], hbuf[:, SEG - 1:SEG]),
                        reads=[bH], writes=[bSTATE])
                    c.op("dve", lambda h, nb=nb: h.tensor_tensor(
                        htil[:], hbuf[:], crep[nb][:], op=OP.mult),
                        reads=[bH, bCR[nb]], writes=[bHT])
                    c.op("dve", lambda h, dt=dt: h.tensor_tensor(
                        yacc[:, dt, :], yacc[:, dt, :], htil[:], op=OP.add),
                        reads=[bHT, bY], writes=[bY])

            for dt in range(NDT):
                c.op("dve", lambda h, dt=dt, sb=sb: h.scalar_tensor_tensor(
                    sv[:, dt, :], xn_d[sb][:, dt, :], dvs[:, dt, 0:1],
                    yacc[:, dt, :], op0=OP.mult, op1=OP.add),
                    reads=[bXND[sb], bDV, bY], writes=[bSV],
                    drain=(dt == NDT - 1))
            for dt in range(NDT):
                c.op("sync", lambda h, dt=dt, t0=t0: h.dma_start(
                    out=o_s[dt * 128:(dt + 1) * 128, t0:t0 + SEG],
                    in_=sv[:, dt, :]), reads=[bSV], dma=True)

        c.finalize([(qn, c.cnt[qn]) for qn in
                    ("qsp", "qact", "qdve", "qpool", "qb0", "qb1", "qc0", "qc1")])
    return nc



def build_L3(C):
    nc = bass.Bass("TRN2", target_bir_lowering=False)
    xnT = nc.dram_tensor("xnT", [T, D], BF16, kind="ExternalInput").ap()
    wgT4 = nc.dram_tensor("wgT4", [4, D, IDIM], F32, kind="ExternalInput").ap()
    wuT4 = nc.dram_tensor("wuT4", [4, D, IDIM], F32, kind="ExternalInput").ap()
    xs_idx = nc.dram_tensor("xs_idx", [4, C], U32, kind="ExternalInput").ap()
    identb = nc.dram_tensor("identb", [128, 128], BF16, kind="ExternalInput").ap()
    o_hmid = nc.dram_tensor("o_hmid", [4 * IDIM, C], BF16, kind="ExternalOutput").ap()

    NFC = (C + 511) // 512
    c = Ctx(nc)
    import contextlib
    with contextlib.ExitStack() as stk:
        sbuf = lambda n, s, d: stk.enter_context(nc.sbuf_tensor(n, s, d))
        psum = lambda n, s, d: stk.enter_context(nc.psum_tensor(n, s, d))
        wg = sbuf("wg", [128, 16, IDIM], BF16)
        wu = sbuf("wu", [128, 16, IDIM], BF16)
        ident = sbuf("ident", [128, 128], BF16)
        ix = [sbuf(f"ix{j}", [128, 1], U32) for j in range(4)]
        xg = [sbuf(f"xg{j}", [128, D], BF16) for j in range(4)]
        xsT = sbuf("xsT", [128, 16, 512], BF16)
        sg = sbuf("sg", [128, 512], F32)
        hm = sbuf("hm", [128, 512], BF16)
        pstb = [psum(f"pstb{i}", [128, 512], BF16) for i in range(2)]
        psg = psum("psg", [128, 512], F32)
        psu = psum("psu", [128, 512], F32)

        bWG, bWU, bID, bXST, bSG, bHM = Buf(), Buf(), Buf(), Buf(), Buf(), Buf()
        bIX = [Buf() for _ in range(4)]
        bXG = [Buf() for _ in range(4)]
        bPTB = [Buf(), Buf()]
        bPSG, bPSU = Buf(), Buf()
        for j in range(4):
            c.add_qsem("qg%d" % j)

        c.op("sync", lambda h: h.dma_start(out=ident[:], in_=identb),
             writes=[bID], dma=True)

        for e in range(4):
            wtsg = c.deps(writes=[bWG])
            wtsu = c.deps(writes=[bWU])
            for dt in range(16):
                tg = c.op("pool", lambda h, e=e, dt=dt: h.dma_start(
                    out=wg[:, dt, :], in_=wgT4[e, dt * 128:(dt + 1) * 128, :]),
                    dma=True, extra=(wtsg if dt == 0 else ()))
                tu = c.op("pool", lambda h, e=e, dt=dt: h.dma_start(
                    out=wu[:, dt, :], in_=wuT4[e, dt * 128:(dt + 1) * 128, :]),
                    dma=True, extra=(wtsu if dt == 0 else ()))
            c.commit(tg, writes=[bWG])
            c.commit(tu, writes=[bWU])
            for fc in range(NFC):
                f0 = fc * 512
                fw = min(512, C - f0)
                npt = (fw + 127) // 128
                for j in range(npt):
                    c.op("sync", lambda h, e=e, f0=f0, j=j: h.dma_start(
                        out=ix[j][:], in_=xs_idx[e, f0 + j * 128: f0 + (j + 1) * 128]
                        .rearrange("(p o) -> p o", o=1)),
                        writes=[bIX[j]], dma=True)
                    c.op("pool", lambda h, j=j: h.indirect_dma_start(
                        out=xg[j][:], out_offset=None, in_=xnT,
                        in_offset=IndirectOffsetOnAxis(ap=ix[j][:, 0:1], axis=0)),
                        reads=[bIX[j]], writes=[bXG[j]], dma=True, q="qg%d" % j)
                for dt in range(16):
                    pb = dt % 2
                    wts = c.deps(reads=[bID] + [bXG[j] for j in range(npt)],
                                 writes=[bPTB[pb]])
                    tok = None
                    for j in range(npt):
                        tok = c.op("pe", lambda h, dt=dt, j=j, pb=pb: h.transpose(
                            pstb[pb][:, j * 128:(j + 1) * 128],
                            xg[j][:, dt * 128:(dt + 1) * 128], ident[:]),
                            extra=wts if j == 0 else ())
                    c.commit(tok, reads=[bID] + [bXG[j] for j in range(npt)],
                             writes=[bPTB[pb]])
                    c.op("dve", lambda h, dt=dt, pb=pb, npt=npt: h.tensor_copy(
                        xsT[:, dt, 0:npt * 128], pstb[pb][:, 0:npt * 128]),
                        reads=[bPTB[pb]], writes=[bXST], drain=(dt == 15))
                for it in range(8):
                    wts = c.deps(reads=[bXST, bWG], writes=[bPSG])
                    tok = None
                    for dt in range(16):
                        tok = c.op("pe", lambda h, it=it, dt=dt, fw=fw: h.matmul(
                            psg[:, 0:fw], wg[:, dt, it * 128:(it + 1) * 128],
                            xsT[:, dt, 0:fw], start=(dt == 0), stop=(dt == 15)),
                            extra=wts if dt == 0 else ())
                    c.commit(tok, reads=[bXST, bWG], writes=[bPSG])
                    wts = c.deps(reads=[bXST, bWU], writes=[bPSU])
                    tok = None
                    for dt in range(16):
                        tok = c.op("pe", lambda h, it=it, dt=dt, fw=fw: h.matmul(
                            psu[:, 0:fw], wu[:, dt, it * 128:(it + 1) * 128],
                            xsT[:, dt, 0:fw], start=(dt == 0), stop=(dt == 15)),
                            extra=wts if dt == 0 else ())
                    c.commit(tok, reads=[bXST, bWU], writes=[bPSU])
                    c.op("act", lambda h, fw=fw: h.activation(
                        sg[:, 0:fw], psg[:, 0:fw], AF.Silu),
                        reads=[bPSG], writes=[bSG], drain=True)
                    c.op("dve", lambda h, fw=fw: h.tensor_tensor(
                        hm[:, 0:fw], sg[:, 0:fw], psu[:, 0:fw], op=OP.mult),
                        reads=[bSG, bPSU], writes=[bHM], drain=True)
                    c.op("sync", lambda h, e=e, it=it, f0=f0, fw=fw: h.dma_start(
                        out=o_hmid[e * IDIM + it * 128: e * IDIM + (it + 1) * 128,
                                   f0:f0 + fw],
                        in_=hm[:, 0:fw]), reads=[bHM], dma=True)

        c.finalize([(qn, c.cnt[qn]) for qn in
                    ("qsp", "qact", "qdve", "qpool", "qg0", "qg1", "qg2", "qg3")])
    return nc



def build_L4(C):
    nc = bass.Bass("TRN2", target_bir_lowering=False)
    hmid = nc.dram_tensor("hmid", [NEXP * IDIM, C], BF16, kind="ExternalInput").ap()
    wdT = nc.dram_tensor("wdT", [NEXP, IDIM, DSH], F32, kind="ExternalInput").ap()
    osc = nc.dram_tensor("osc", [NEXP, C], U32, kind="ExternalInput").ap()
    w4v_d = nc.dram_tensor("w4v", [128, 4, T // 128], F32, kind="ExternalInput").ap()
    s_sh = nc.dram_tensor("s_sh", [T, DSH], F32, kind="ExternalInput").ap()
    xdsh = nc.dram_tensor("xdsh", [T, DSH], F32, kind="ExternalInput").ap()
    o_out = nc.dram_tensor("o_out", [T, DSH], F32, kind="ExternalOutput").ap()
    O2 = nc.dram_tensor("O2", [T * 4 + 128, DSH], F32).ap()

    NPT = (C + 127) // 128
    TT = 8
    c = Ctx(nc)
    import contextlib
    with contextlib.ExitStack() as stk:
        sbuf = lambda n, s, d: stk.enter_context(nc.sbuf_tensor(n, s, d))
        psum = lambda n, s, d: stk.enter_context(nc.psum_tensor(n, s, d))
        hs = [sbuf(f"hs{i}", [128, 8, C], BF16) for i in range(2)]
        wd = [sbuf(f"wd{i}", [128, 8, DSH], BF16) for i in range(2)]
        ot = [sbuf(f"ot{i}", [128, DSH], F32) for i in range(2)]
        oix = [sbuf(f"oix{i}", [128, 1], U32) for i in range(2)]
        pso = [psum(f"pso{i}", [128, DSH], F32) for i in range(2)]
        o2t = [sbuf(f"o2t{i}", [128, TT, 4 * DSH], F32) for i in range(2)]
        wv = sbuf("wv", [128, 4, T // 128], F32)
        st_ = [sbuf(f"st_{i}", [128, TT, DSH], F32) for i in range(2)]
        xt_ = [sbuf(f"xt_{i}", [128, TT, DSH], F32) for i in range(2)]
        ym = sbuf("ym", [128, DSH], F32)
        gt = sbuf("gt", [128, DSH], F32)
        outt = sbuf("outt", [128, TT, DSH], F32)

        bHS = [Buf(), Buf()]
        bWD = [Buf(), Buf()]
        bOT = [Buf(), Buf()]
        bOIX = [Buf(), Buf()]
        bPSO = [Buf(), Buf()]
        bO2 = Buf()
        bO2T = [Buf(), Buf()]
        bWV = Buf()
        bST = [Buf(), Buf()]
        bXT = [Buf(), Buf()]
        bYM, bGT, bOUT = Buf(), Buf(), Buf()
        for j in range(2):
            c.add_qsem("qh%d" % j)
            c.add_qsem("qsc%d" % j)

        for e in range(NEXP):
            eb = e % 2
            wtsh = c.deps(writes=[bHS[eb]])
            th = None
            for it in range(8):
                th = c.op("sync", lambda h, e=e, it=it, eb=eb: h.dma_start(
                    out=hs[eb][:, it, :],
                    in_=hmid[e * IDIM + it * 128:e * IDIM + (it + 1) * 128, :]),
                    dma=True, extra=(wtsh if it == 0 else ()), q="qh%d" % eb)
            c.commit(th, writes=[bHS[eb]])
            c.op("pool", lambda h, e=e, eb=eb: h.dma_start(
                out=wd[eb][:], in_=wdT[e].rearrange("(it p) c -> p it c", p=128)),
                writes=[bWD[eb]], dma=True)
            for pt in range(NPT):
                p0 = pt * 128
                pw = min(128, C - p0)
                ob = pt % 2
                c.op("sync", lambda h, e=e, p0=p0, ob=ob: h.dma_start(
                    out=oix[ob][:], in_=osc[e, p0:p0 + 128]
                    .rearrange("(p o) -> p o", o=1)),
                    writes=[bOIX[ob]], dma=True)
                wts = c.deps(reads=[bHS[eb], bWD[eb]], writes=[bPSO[ob]])
                tok = None
                for it in range(8):
                    tok = c.op("pe", lambda h, it=it, p0=p0, pw=pw, ob=ob, eb=eb:
                               h.matmul(
                        pso[ob][0:pw, :], hs[eb][:, it, p0:p0 + pw], wd[eb][:, it, :],
                        start=(it == 0), stop=(it == 7)),
                        extra=wts if it == 0 else ())
                c.commit(tok, reads=[bHS[eb], bWD[eb]], writes=[bPSO[ob]])
                c.op("act", lambda h, ob=ob, pw=pw: h.copy(
                    ot[ob][0:pw, :], pso[ob][0:pw, :]),
                    reads=[bPSO[ob]], writes=[bOT[ob]], drain=True)
                c.op("pool", lambda h, ob=ob, pw=pw: h.indirect_dma_start(
                    out=O2, out_offset=IndirectOffsetOnAxis(
                        ap=oix[ob][:, 0:1], axis=0),
                    in_=ot[ob][0:pw, :], in_offset=None),
                    reads=[bOT[ob], bOIX[ob]], writes=[bO2], dma=True,
                    q="qsc%d" % ob)

        c.op("sync", lambda h: h.dma_start(out=wv[:], in_=w4v_d),
             writes=[bWV], dma=True)

        for ch in range(T // (128 * TT)):
            cb = ch % 2
            r0 = ch * 128 * TT
            c.op("sync", lambda h, r0=r0, cb=cb: h.dma_start(
                out=o2t[cb][:], in_=O2[r0 * 4:(r0 + 128 * TT) * 4, :].rearrange(
                    "(tt p s) c -> p tt (s c)", p=128, s=4)),
                reads=[bO2], writes=[bO2T[cb]], dma=True,
                extra=[("qsc0", c.cnt["qsc0"]), ("qsc1", c.cnt["qsc1"])])
            c.op("sync", lambda h, r0=r0, cb=cb: h.dma_start(
                out=st_[cb][:], in_=s_sh[r0:r0 + 128 * TT, :].rearrange(
                    "(tt p) c -> p tt c", p=128)), writes=[bST[cb]], dma=True)
            c.op("sync", lambda h, r0=r0, cb=cb: h.dma_start(
                out=xt_[cb][:], in_=xdsh[r0:r0 + 128 * TT, :].rearrange(
                    "(tt p) c -> p tt c", p=128)), writes=[bXT[cb]], dma=True)
            for tt in range(TT):
                ti = ch * TT + tt
                c.op("dve", lambda h, tt=tt, ti=ti, cb=cb: h.tensor_scalar(
                    ym[:], o2t[cb][:, tt, 0:DSH], wv[:, 0, ti:ti + 1], None,
                    op0=OP.mult), reads=[bO2T[cb], bWV], writes=[bYM])
                for sl in range(1, 4):
                    c.op("dve", lambda h, tt=tt, ti=ti, sl=sl, cb=cb:
                         h.scalar_tensor_tensor(
                        ym[:], o2t[cb][:, tt, sl * DSH:(sl + 1) * DSH],
                        wv[:, sl, ti:ti + 1], ym[:],
                        op0=OP.mult, op1=OP.add),
                        reads=[bO2T[cb], bWV, bYM], writes=[bYM],
                        drain=(sl == 3))
                c.op("act", lambda h: h.activation(gt[:], ym[:], AF.Silu),
                     reads=[bYM], writes=[bGT], drain=True)
                c.op("dve", lambda h, tt=tt, cb=cb: h.tensor_tensor(
                    outt[:, tt, :], st_[cb][:, tt, :], gt[:], op=OP.mult),
                    reads=[bST[cb], bGT], writes=[bOUT])
                c.op("dve", lambda h, tt=tt, cb=cb: h.tensor_tensor(
                    outt[:, tt, :], outt[:, tt, :], xt_[cb][:, tt, :], op=OP.add),
                    reads=[bXT[cb], bOUT], writes=[bOUT], drain=(tt == TT - 1))
            c.op("sync", lambda h, r0=r0: h.dma_start(
                out=o_out[r0:r0 + 128 * TT, :].rearrange(
                    "(tt p) c -> p tt c", p=128), in_=outt[:]),
                reads=[bOUT], dma=True)

        c.finalize([(qn, c.cnt[qn]) for qn in
                    ("qsp", "qact", "qdve", "qpool", "qh0", "qh1", "qsc0", "qsc1")])
    return nc



def _routing(topk_idx):
    flat = np.asarray(topk_idx, dtype=np.int64).reshape(-1)
    order = np.argsort(flat, kind="stable")
    counts = np.bincount(flat, minlength=NEXP)
    C = int(max(256, ((counts.max() + 255) // 256) * 256))
    xs_tok = np.zeros((NEXP, C), np.uint32)
    osc_idx = np.zeros((NEXP, C), np.uint32)
    jj = np.arange(C)
    osc_idx[:, :] = (T * 4 + (jj % 128)).astype(np.uint32)[None, :]
    ptr = 0
    for e in range(NEXP):
        cnt = int(counts[e])
        idxs = order[ptr:ptr + cnt]
        ptr += cnt
        toks = idxs // 4
        slots = idxs % 4
        xs_tok[e, :cnt] = toks.astype(np.uint32)
        osc_idx[e, :cnt] = (toks * 4 + slots).astype(np.uint32)
    return C, xs_tok, osc_idx


def kernel(L0_attn_x, L0_attn_W_x, L0_attn_W_dt, L0_attn_b_dt, L0_attn_A_log,
           L0_attn_D, L0_ffn_gate_weight, L0_ffn_up_weight, L0_ffn_down_weight,
           L0_ffn_topk_idx, L0_ffn_topk_weight):
    x = np.ascontiguousarray(np.asarray(L0_attn_x, np.float32).reshape(T, D))
    wxT = np.ascontiguousarray(np.asarray(L0_attn_W_x, np.float32).T)
    wdtT = np.ascontiguousarray(np.asarray(L0_attn_W_dt, np.float32).T)
    bdt = np.asarray(L0_attn_b_dt, np.float32).reshape(D, 1)
    alog = np.asarray(L0_attn_A_log, np.float32)
    dvec = np.asarray(L0_attn_D, np.float32).reshape(D, 1)
    gw = np.asarray(L0_ffn_gate_weight, np.float32)
    uw = np.asarray(L0_ffn_up_weight, np.float32)
    dw = np.asarray(L0_ffn_down_weight, np.float32)
    tki = np.asarray(L0_ffn_topk_idx)
    tkw = np.asarray(L0_ffn_topk_weight, np.float32)

    identb = np.eye(128, dtype=ml_dtypes.bfloat16)
    cores = list(range(NC))

    nc1 = build_L1()
    maps1 = [dict(x=np.ascontiguousarray(x[k * TSH:(k + 1) * TSH]),
                  xT=np.ascontiguousarray(x[k * TSH:(k + 1) * TSH].T),
                  wxT=wxT) for k in range(NC)]
    r1 = run_bass_kernel_spmd(nc1, maps1, core_ids=cores).results
    xn_full = np.concatenate([r["o_xn"] for r in r1], axis=0)
    xnT_b = np.concatenate([r["o_xnb"] for r in r1], axis=0)
    xpf = np.concatenate([r["o_xp"] for r in r1], axis=1)
    xpf = np.ascontiguousarray(xpf)
    xnd_full = np.ascontiguousarray(xn_full.T)

    nc2 = build_L2()
    maps2 = []
    for k in range(NC):
        dsl = slice(k * DSH, (k + 1) * DSH)
        maps2.append(dict(
            xnd=np.ascontiguousarray(xnd_full[dsl]), xp=xpf,
            wdtT=np.ascontiguousarray(wdtT[:, dsl]),
            bdt=np.ascontiguousarray(
                bdt[dsl].reshape(DSH // 128, 128, 1).transpose(1, 0, 2)),
            alog=np.ascontiguousarray(
                alog[dsl].reshape(DSH // 128, 128, NST).transpose(1, 0, 2)),
            dvec=np.ascontiguousarray(
                dvec[dsl].reshape(DSH // 128, 128, 1).transpose(1, 0, 2))))
    r2 = run_bass_kernel_spmd(nc2, maps2, core_ids=cores).results
    s_full = np.concatenate([r["o_s"] for r in r2], axis=0)
    s_full = np.ascontiguousarray(s_full.T)

    C, xs_tok, osc_idx = _routing(tki)
    nc3 = build_L3(C)
    maps3 = []
    for k in range(NC):
        es = slice(4 * k, 4 * k + 4)
        maps3.append(dict(
            xnT=xnT_b,
            wgT4=np.ascontiguousarray(np.swapaxes(gw[es], 1, 2)),
            wuT4=np.ascontiguousarray(np.swapaxes(uw[es], 1, 2)),
            xs_idx=np.ascontiguousarray(xs_tok[es]),
            identb=identb))
    r3 = run_bass_kernel_spmd(nc3, maps3, core_ids=cores).results
    hmid = np.concatenate([r["o_hmid"] for r in r3], axis=0)
    hmid = np.ascontiguousarray(hmid)

    w4v = np.ascontiguousarray(
        tkw.reshape(T // 128, 128, 4).transpose(1, 2, 0))
    nc4 = build_L4(C)
    maps4 = []
    for k in range(NC):
        dsl = slice(k * DSH, (k + 1) * DSH)
        maps4.append(dict(
            hmid=hmid,
            wdT=np.ascontiguousarray(np.swapaxes(dw, 1, 2)[:, :, dsl]),
            osc=osc_idx, w4v=w4v,
            s_sh=np.ascontiguousarray(s_full[:, dsl]),
            xdsh=np.ascontiguousarray(x[:, dsl])))
    r4 = run_bass_kernel_spmd(nc4, maps4, core_ids=cores).results
    out = np.concatenate([r["o_out"] for r in r4], axis=1)
    return out.reshape(1, T, D)
